# version 03 (final)
# baseline (speedup 1.0000x reference)
import numpy as np
import ml_dtypes
import concourse.bass as bass
import concourse.mybir as mybir
from concourse.bass_utils import run_bass_kernel_spmd
from contextlib import ExitStack

F32 = mybir.dt.float32
BF16 = mybir.dt.bfloat16
AF = mybir.ActivationFunctionType
ALU = mybir.AluOpType
AX = mybir.AxisListType
bf16 = ml_dtypes.bfloat16

T = 2048
D = 2048
NH = 16
HD = 128
HID = 8192
NSEQ = 2
NTB = T // 128
NDB = D // 128
NJB = HID // 128
ISQ = float(1.0 / np.sqrt(np.float32(HD)))
F32_EPS = float(np.finfo(np.float32).eps)
NEG = float(np.finfo(np.float32).min)

ENGINES = ("sync", "tensor", "scalar", "vector", "gpsimd")
OWN = {"tensor": "cP", "scalar": "cS", "vector": "cV", "gpsimd": "cG"}


class K:
    def __init__(self, nc, stack):
        self.nc = nc
        self.stack = stack
        self.sem = {}
        self.n = {}
        self.dma_sems = []
        self.trace = []
        for name in ("cP", "cS", "cV", "cG"):
            self.sem[name] = stack.enter_context(nc.semaphore(name))
            self.n[name] = 0

    def dsem(self, name):
        if name not in self.sem:
            self.sem[name] = self.stack.enter_context(self.nc.semaphore(name))
            self.n[name] = 0
            self.dma_sems.append(name)
        return name


class Plan:

    def __init__(self, k: K):
        self.k = k
        self.ops = {e: [] for e in ENGINES}

    def op(self, eng, fn, waits=(), inc=None, k=1):
        w = []
        for item in waits:
            if item is None:
                continue
            s, v = item
            if v:
                w.append((s, int(v)))
        val = None
        if inc is not None:
            self.k.n[inc] += k
            val = self.k.n[inc]
        self.ops[eng].append((fn, w, inc, k))
        return val

    def dma(self, eng, sem, dst, src, waits=()):
        q = self.k.dsem(sem)
        v = self.op(eng, lambda E, d=dst, s=src: E.dma_start(d, s),
                    waits=waits, inc=q, k=16)
        return (q, v)

    def emit(self):
        nc = self.k.nc
        rec = {}
        for eng in ENGINES:
            hw = {}
            rows = []
            for fn, w, inc, kk in self.ops[eng]:
                fw = []
                for s, v in w:
                    if hw.get(s, 0) >= v:
                        continue
                    fw.append((s, v))
                    hw[s] = v
                rows.append((fw, inc, kk, fn))
            rec[eng] = rows
        self.k.trace.append(rec)
        with nc.Block() as blk:
            for eng in ENGINES:
                if not rec[eng]:
                    continue

                def mk(rows=rec[eng]):
                    def build(E):
                        for fw, inc, kk, fn in rows:
                            for s, v in fw:
                                E.wait_ge(self.k.sem[s], v)
                            ins = fn(E)
                            if inc is not None:
                                ins.then_inc(self.k.sem[inc], kk)
                    return build

                getattr(blk, eng)(mk())


def check_deadlock(k):
    sems = {s: 0 for s in k.n}
    for pi, rec in enumerate(k.trace):
        heads = {e: 0 for e in ENGINES}
        progress = True
        while progress:
            progress = False
            for e in ENGINES:
                rows = rec[e]
                while heads[e] < len(rows):
                    fw, inc, kk, fn = rows[heads[e]]
                    if all(sems.get(s, 0) >= v for s, v in fw):
                        if inc is not None:
                            sems[inc] = sems.get(inc, 0) + kk
                        heads[e] += 1
                        progress = True
                    else:
                        break
        stuck = {e: heads[e] for e in ENGINES if heads[e] < len(rec[e])}
        if stuck:
            msg = []
            for e, h in stuck.items():
                fw = rec[e][h][0]
                unmet = [(s, v, sems.get(s, 0)) for s, v in fw
                         if sems.get(s, 0) < v]
                msg.append(f"plan {pi} eng {e} op {h}/{len(rec[e])} "
                           f"unmet={unmet}")
            raise RuntimeError("DEADLOCK: " + "; ".join(msg))
    for s, v in k.n.items():
        if sems.get(s, 0) != v:
            raise RuntimeError(f"sem {s} final {sems.get(s, 0)} != planned {v}")


I32 = mybir.dt.int32
RSQRT_MAGIC = 0x5F3759DF


def plan_rsqrt(p, v_ap, out_ap, t1_ap, t2_ap, first_waits):
    vi = v_ap.bitcast(I32)
    oi = out_ap.bitcast(I32)
    t1i = t1_ap.bitcast(I32)

    def step(fn, w):
        return p.op("vector", fn, waits=w, inc="cV")

    last = step(lambda E, o=t1i, a=vi:
                E.tensor_single_scalar(o, a, 1, ALU.logical_shift_right),
                first_waits)
    last = step(lambda E, o=t1i, a=t1i:
                E.tensor_single_scalar(o, a, 0xFFFFFFFF, ALU.bitwise_xor),
                [("cV", last)])
    last = step(lambda E, o=oi, a=t1i:
                E.tensor_single_scalar(o, a, RSQRT_MAGIC + 1, ALU.add),
                [("cV", last)])
    for _ in range(3):
        last = step(lambda E, o=t1_ap, a=out_ap:
                    E.tensor_tensor(o, a, a, ALU.mult), [("cV", last)])
        last = step(lambda E, o=t2_ap, a=v_ap, b=t1_ap:
                    E.tensor_tensor(o, a, b, ALU.mult), [("cV", last)])
        last = step(lambda E, o=t2_ap, a=t2_ap:
                    E.tensor_scalar(o, a, -0.5, 1.5, ALU.mult, ALU.add),
                    [("cV", last)])
        last = step(lambda E, o=out_ap, a=out_ap, b=t2_ap:
                    E.tensor_tensor(o, a, b, ALU.mult), [("cV", last)])
    return last


def build_nc(upto=3, p2_heads=NH, p2_wo=True, p2_seqs=NSEQ, p2_stage=5, p2_soft=4, p2_dump=False):
    nc = bass.Bass("TRN2", target_bir_lowering=False)
    stack = ExitStack()

    x_d = nc.dram_tensor("x", [NSEQ, T, D], F32, kind="ExternalInput").ap()
    wq_d = nc.dram_tensor("wq", [NH, D, HD], F32, kind="ExternalInput").ap()
    wk_d = nc.dram_tensor("wk", [NH, D, HD], F32, kind="ExternalInput").ap()
    wv_d = nc.dram_tensor("wv", [NH, D, HD], F32, kind="ExternalInput").ap()
    wo_d = nc.dram_tensor("wo", [4, D, 512], F32, kind="ExternalInput").ap()
    wfc_d = nc.dram_tensor("wfc", [HID, D], F32, kind="ExternalInput").ap()
    wpj_d = nc.dram_tensor("wpj", [D, HID], F32, kind="ExternalInput").ap()
    ident_d = nc.dram_tensor("ident", [128, 128], BF16, kind="ExternalInput").ap()
    mask_d = nc.dram_tensor("mask", [128, 128], F32, kind="ExternalInput").ap()
    onesb_d = nc.dram_tensor("onesb", [128, 1], BF16, kind="ExternalInput").ap()
    onesf_d = nc.dram_tensor("onesf", [1, 1], F32, kind="ExternalInput").ap()
    epsc_d = nc.dram_tensor("epsc", [1, 1], F32, kind="ExternalInput").ap()
    if upto >= 3:
        y_d = nc.dram_tensor("y", [NSEQ, T, D], F32, kind="ExternalOutput").ap()

    wk1 = "Internal" if upto > 1 else "ExternalOutput"
    wk2 = "Internal" if upto > 2 else "ExternalOutput"
    wfct_d = nc.dram_tensor("wfct", [NJB, 128, D], BF16, kind=wk1).ap()
    wpjt_d = nc.dram_tensor("wpjt", [NJB, 128, D], BF16, kind=wk1).ap()
    zkind = "ExternalOutput" if (upto == 2 and not p2_wo) else "Internal"
    z_d = nc.dram_tensor("zscr", [NSEQ, 4, NH, 128, 512], BF16, kind=zkind).ap()
    if p2_dump:
        sdump_d = nc.dram_tensor("sdump", [4, 128, T], F32, kind="ExternalOutput").ap()
        pdump_d = nc.dram_tensor("pdump", [4, 128, T], BF16, kind="ExternalOutput").ap()
        qdump_d = nc.dram_tensor("qdump", [128, T], BF16, kind="ExternalOutput").ap()
        kdump_d = nc.dram_tensor("kdump", [128, T], BF16, kind="ExternalOutput").ap()
        xdump_d = nc.dram_tensor("xdump", [128, NDB, T], BF16, kind="ExternalOutput").ap()
    x1_d = nc.dram_tensor("x1scr", [NSEQ, T, D], F32, kind=wk2).ap()

    def finish(k):
        p = Plan(k)
        p.op("sync", lambda E: E.nop(),
             waits=[(s, k.n[s]) for s in k.dma_sems])
        p.emit()
        check_deadlock(k)
        stack.close()
        return nc

    ident = nc.alloc_sbuf_tensor("identt", [128, 128], BF16)
    maskt = nc.alloc_sbuf_tensor("maskt", [128, 128], F32)
    onesb = nc.alloc_sbuf_tensor("onesbt", [128, 1], BF16)
    onesf = nc.alloc_sbuf_tensor("onesft", [1, 1], F32)
    epsc = nc.alloc_sbuf_tensor("epsct", [1, 1], F32)

    k = K(nc, stack)

    p = Plan(k)
    p.dma("sync", "cst", ident[:], ident_d)
    p.dma("sync", "cst", maskt[:], mask_d)
    p.dma("sync", "cst", onesb[:], onesb_d)
    p.dma("sync", "cst", onesf[:], onesf_d)
    ld0 = p.dma("sync", "cst", epsc[:], epsc_d)
    p.op("scalar", lambda E: E.preload_act_table(AF.Exp), waits=[ld0])
    p.emit()

    with ExitStack() as ps:
        wb2 = ps.enter_context(nc.sbuf_tensor("wb2", [128, 2, HID], BF16))
        r_t = ps.enter_context(nc.sbuf_tensor("r_t", [128, HID], BF16))
        gtp = ps.enter_context(nc.sbuf_tensor("gtp", [128, HID], BF16))
        gtn = ps.enter_context(nc.sbuf_tensor("gtn", [128, HID], BF16))
        q_t = ps.enter_context(nc.sbuf_tensor("q_t", [128, HID], BF16))
        wt_t = ps.enter_context(nc.sbuf_tensor("wt_t", [128, HID], BF16))
        stg2 = ps.enter_context(nc.sbuf_tensor("stg2", [128, 2, HID], BF16))
        gs_t = ps.enter_context(nc.sbuf_tensor("gs_t", [128, 128], F32))
        sc_t = ps.enter_context(nc.sbuf_tensor("sc_t", [128, 128], BF16))
        rec_t = ps.enter_context(nc.sbuf_tensor("rec_t", [128, 128], F32))
        ptr0 = ps.enter_context(nc.psum_tensor("ptr0", [128, 1024], BF16))
        ptr1 = ps.enter_context(nc.psum_tensor("ptr1", [128, 1024], BF16))
        ptr = [ptr0, ptr1]

        items = [("fc", j) for j in range(NJB)] + [("pj", b) for b in range(NDB)]
        p = Plan(k)
        wb_guard = [[], []]
        stg_guard = [[], []]
        ptr_guard = [[], []]
        prev_tr = None

        def prep_src(i):
            kind, j = items[i]
            W = D if kind == "fc" else HID
            src = (wfc_d if kind == "fc" else wpj_d)[j * 128:(j + 1) * 128, :]
            return src, W

        dG_at = {}
        src0, W0 = prep_src(0)
        dG_at[0] = p.dma("gpsimd", "wb0", wb2[:, 0, 0:W0], src0)

        for i, (kind, j) in enumerate(items):
            W = D if kind == "fc" else HID
            G = W // 64
            NT = W // 128
            par = i % 2
            if i + 1 < len(items):
                nsrc, nW = prep_src(i + 1)
                dG_at[i + 1] = p.dma("gpsimd", f"wb{(i + 1) % 2}",
                                     wb2[:, (i + 1) % 2, 0:nW], nsrc,
                                     waits=wb_guard[(i + 1) % 2])
            wb = wb2[:, par, 0:W]
            wb3 = wb.rearrange("p (g e) -> p g e", e=64)
            v1 = p.op("vector", lambda E, o=gs_t[:, 0:G], a=wb3:
                      E.reduce_sum(o, a, axis=AX.X, apply_absolute_value=True),
                      waits=[dG_at[i]], inc="cV")
            v2 = p.op("vector", lambda E, o=sc_t[:, 0:G], a=gs_t[:, 0:G]:
                      E.tensor_scalar(o, a, 1.0 / 64.0, 1e-8, ALU.mult, ALU.max),
                      waits=[("cV", v1)], inc="cV")
            v3 = p.op("vector", lambda E, o=rec_t[:, 0:G], a=sc_t[:, 0:G]:
                      E.reciprocal(o, a), waits=[("cV", v2)], inc="cV")
            scb = sc_t[:, 0:G].unsqueeze(2).broadcast_to([128, G, 64])
            recb = rec_t[:, 0:G].unsqueeze(2).broadcast_to([128, G, 64])
            vr = p.op("vector", lambda E, o=r_t[:, 0:W].rearrange("p (g e) -> p g e", e=64),
                      a=wb3, b=recb: E.tensor_tensor(o, a, b, ALU.mult),
                      waits=[("cV", v3)], inc="cV")
            wb_guard[par] = [("cV", vr)]
            p.op("vector", lambda E, o=gtp[:, 0:W], a=r_t[:, 0:W]:
                 E.tensor_single_scalar(o, a, 0.5, ALU.is_gt),
                 waits=[("cV", vr)])
            vg = p.op("vector", lambda E, o=gtn[:, 0:W], a=r_t[:, 0:W]:
                      E.tensor_single_scalar(o, a, -0.5, ALU.is_lt), inc="cV")
            vq = p.op("vector", lambda E, o=q_t[:, 0:W], a=gtp[:, 0:W], b=gtn[:, 0:W]:
                      E.tensor_tensor(o, a, b, ALU.subtract),
                      waits=[("cV", vg)], inc="cV")
            vw = p.op("vector", lambda E, o=wt_t[:, 0:W].rearrange("p (g e) -> p g e", e=64),
                      a=q_t[:, 0:W].rearrange("p (g e) -> p g e", e=64), b=scb:
                      E.tensor_tensor(o, a, b, ALU.mult),
                      waits=([("cP", prev_tr)] if prev_tr else []) + [("cV", vq)],
                      inc="cV")
            last_copy = None
            for g in range(NT // 4):
                bpar = g % 2
                tw = [("cV", vw)] + ptr_guard[bpar]
                pe = None
                for t in range(4):
                    tt = 4 * g + t
                    pe = p.op("tensor", lambda E, o=ptr[bpar][:, t * 128:(t + 1) * 128],
                              a=wt_t[:, tt * 128:(tt + 1) * 128]:
                              E.transpose(o, a, ident[:]),
                              waits=tw if t == 0 else (), inc="cP" if t == 3 else None)
                cw = [("cP", pe)] + stg_guard[par]
                last_copy = p.op("scalar", lambda E, o=stg2[:, par, g * 512:(g + 1) * 512],
                                 a=ptr[bpar][:, 0:512]: E.copy(o, a),
                                 waits=cw, inc="cS")
                ptr_guard[bpar] = [("cS", last_copy)]
                prev_tr = pe
            if kind == "fc":
                dst = wfct_d[j]
                srcS = stg2[:, par, 0:D]
            else:
                dst = wpjt_d[:, :, j * 128:(j + 1) * 128].rearrange("j p o -> p j o")
                srcS = stg2[:, par, 0:HID].rearrange("p (j o) -> p j o", o=128)
            dv = p.dma("sync", f"sg{par}", dst, srcS, waits=[("cS", last_copy)])
            stg_guard[par] = [dv]
        prep_done = [(s, k.n[s]) for s in ("sg0", "sg1") if s in k.n]
        p.emit()

    if upto == 1:
        return finish(k)

    def ln_block(p, src_tile_fn, n_tb, dst_fm, dst_col_off, sm, xjunk, xn_tm2,
                 ptr, ptr_guard, xt_guard):
        ssq, negm, msq, lnb, lnv, rstd = sm
        last_cp = None
        cp_hist = {}
        for tb in range(n_tb):
            par = tb % 2
            xt, xw = src_tile_fn(tb)
            sq = p.op("scalar", lambda E, o=xjunk[:], a=xt, acc=ssq[:, par:par + 1]:
                      E.activation(o, a, AF.Square, accum_out=acc),
                      waits=xw, inc="cS")
            n1 = p.op("vector", lambda E, o=negm[:, par:par + 1], a=xt:
                      E.reduce_sum(o, a, axis=AX.X, negate=True), waits=xw,
                      inc="cV")
            n2 = p.op("vector", lambda E, o=negm[:, par:par + 1], a=negm[:, par:par + 1]:
                      E.tensor_single_scalar(o, a, 1.0 / D, ALU.mult),
                      waits=[("cV", n1)], inc="cV")
            n3 = p.op("vector", lambda E, o=msq[:, par:par + 1], a=negm[:, par:par + 1],
                      b=negm[:, par:par + 1]: E.tensor_tensor(o, a, b, ALU.mult),
                      waits=[("cV", n2)], inc="cV")
            vl = p.op("vector", lambda E, o=lnb[:, par:par + 1], a=msq[:, par:par + 1]:
                      E.tensor_scalar(o, a, -1.0, 1e-5, ALU.mult, ALU.add),
                      waits=[("cV", n3)], inc="cV")
            t1 = p.op("vector", lambda E, o=lnv[:, par:par + 1], a=ssq[:, par:par + 1]:
                      E.tensor_single_scalar(o, a, 1.0 / D, ALU.mult),
                      waits=[("cV", vl), ("cS", sq)], inc="cV")
            t2 = p.op("vector", lambda E, o=lnv[:, par:par + 1], a=lnv[:, par:par + 1],
                      b=lnb[:, par:par + 1]: E.tensor_tensor(o, a, b, ALU.add),
                      waits=[("cV", t1)], inc="cV")
            se = plan_rsqrt(p, lnv[:, par:par + 1], rstd[:, par:par + 1],
                            msq[:, par:par + 1], lnb[:, par:par + 1],
                            [("cV", t2)])
            va_w = [("cV", se)]
            if tb - 2 in cp_hist:
                va_w.append(("cS", cp_hist[tb - 2]))
            va = p.op("vector", lambda E, o=xn_tm2[:, par, :], a=xt,
                      s1=negm[:, par:par + 1], s2=rstd[:, par:par + 1]:
                      E.tensor_scalar(o, a, s1, s2, ALU.add, ALU.mult),
                      waits=va_w, inc="cV")
            if xt_guard is not None:
                xt_guard[par] = [("cV", va)]
            col = dst_col_off(tb)
            for g in range(4):
                bpar = (tb * 4 + g) % 2
                tw = [("cV", va)] + ptr_guard[bpar]
                pe = None
                for t in range(4):
                    i = 4 * g + t
                    pe = p.op("tensor", lambda E, o=ptr[bpar][:, t * 128:(t + 1) * 128],
                              a=xn_tm2[:, par, i * 128:(i + 1) * 128]:
                              E.transpose(o, a, ident[:]),
                              waits=tw if t == 0 else (), inc="cP" if t == 3 else None)
                last_cp = p.op("scalar",
                               lambda E, o=dst_fm[:, 4 * g:4 * g + 4, col:col + 128],
                               a=ptr[bpar][:, 0:512].rearrange("p (a b) -> p a b", b=128):
                               E.copy(o, a), waits=[("cP", pe)], inc="cS")
                ptr_guard[bpar] = [("cS", last_cp)]
        return last_cp

    def fence(p):
        w = [(s, k.n[s]) for s in k.dma_sems]
        for eng in ("sync", "scalar", "gpsimd"):
            p.op(eng, lambda E: E.nop(), waits=w)

    with ExitStack() as pw:
        wsl2 = pw.enter_context(nc.sbuf_tensor("wsl2", [128, 2, 3, NDB, 128], BF16))

        wsl_dma = {}
        z_dma = {}
        zslot_guard = [[], []]
        x1_done = {}

        def prefetch_wsl(p, seq, h):
            par = h % 2
            v = None
            for m, wsrc in enumerate((wq_d, wk_d, wv_d)):
                v = p.dma("gpsimd", f"ws{par}", wsl2[:, par, m, :, :],
                          wsrc[h].rearrange("(i p) j -> p i j", p=128))
            wsl_dma[(seq, h)] = v

        for seq in range(p2_seqs):
          with ExitStack() as ps:
            xn_fm = ps.enter_context(nc.sbuf_tensor(f"xn_fm_{seq}", [128, NDB, T], BF16))
            q_sb = ps.enter_context(nc.sbuf_tensor(f"q_sb_{seq}", [128, T], BF16))
            k_sb = ps.enter_context(nc.sbuf_tensor(f"k_sb_{seq}", [128, T], BF16))
            vf_sb = ps.enter_context(nc.sbuf_tensor(f"vf_sb_{seq}", [128, T], BF16))
            v_sb = ps.enter_context(nc.sbuf_tensor(f"v_sb_{seq}", [128, NTB, 128], BF16))
            s_f = ps.enter_context(nc.sbuf_tensor(f"s_f_{seq}", [128, T], F32))
            p_f = ps.enter_context(nc.sbuf_tensor(f"p_f_{seq}", [128, T], F32))
            p_b = ps.enter_context(nc.sbuf_tensor(f"p_b_{seq}", [128, T], BF16))
            pT = ps.enter_context(nc.sbuf_tensor(f"pT_{seq}", [128, NTB, 512], BF16))
            z_t2 = ps.enter_context(nc.sbuf_tensor(f"z_t2_{seq}", [128, 2, 512], BF16))
            xt2 = ps.enter_context(nc.sbuf_tensor(f"xt2_{seq}", [128, 2, D], F32))
            xjunk = ps.enter_context(nc.sbuf_tensor(f"xjunk_{seq}", [128, D], F32))
            xn_tm2 = ps.enter_context(nc.sbuf_tensor(f"xn_tm2_{seq}", [128, 2, D], BF16))
            sm = [ps.enter_context(nc.sbuf_tensor(f"lnsm{i}_{seq}", [128, 2], F32))
                  for i in range(6)]
            smax = ps.enter_context(nc.sbuf_tensor(f"smax_{seq}", [128, 2], F32))
            ssum = ps.enter_context(nc.sbuf_tensor(f"ssum_{seq}", [128, 2], F32))
            srl = ps.enter_context(nc.sbuf_tensor(f"srl_{seq}", [128, 2], F32))
            psA = ps.enter_context(nc.psum_tensor(f"psA_{seq}", [128, 512], F32))
            psB = ps.enter_context(nc.psum_tensor(f"psB_{seq}", [128, 512], F32))
            psZ0 = ps.enter_context(nc.psum_tensor(f"psZ0_{seq}", [128, 512], F32))
            psZ1 = ps.enter_context(nc.psum_tensor(f"psZ1_{seq}", [128, 512], F32))
            ptrA = ps.enter_context(nc.psum_tensor(f"ptrA_{seq}", [128, 1024], BF16))
            ptrB = ps.enter_context(nc.psum_tensor(f"ptrB_{seq}", [128, 1024], BF16))
            psQK = [psA, psB]
            psZ2 = [psZ0, psZ1]
            ptr = [ptrA, ptrB]

            p = Plan(k)
            fence(p)
            xt_guard = [[], []]
            ptr_guard = [[], []]

            def src_tile(tb, seq=seq, p=p, xt_guard=xt_guard):
                par = tb % 2
                dv = p.dma("sync", f"xi{par}", xt2[:, par, :],
                           x_d[seq, tb * 128:(tb + 1) * 128, :], waits=xt_guard[par])
                return xt2[:, par, :], [dv]

            ln_block(p, src_tile, NTB, xn_fm, lambda tb: tb * 128,
                     sm, xjunk, xn_tm2, ptr, ptr_guard, xt_guard)
            prefetch_wsl(p, seq, 0)
            p.emit()

            for h in range(p2_heads):
                p = Plan(k)
                if h + 1 < p2_heads:
                    prefetch_wsl(p, seq, h + 1)
                wslv = wsl_dma[(seq, h)]
                par = h % 2
                dump_guard = []
                bank_guard = [[], []]
                ptr_guard = [[], []]
                zps_guard = [[], []]

                def qkv(mat, dst):
                    last = None
                    for tc in range(4):
                        b = (mat * 4 + tc) % 2
                        w = [wslv] + bank_guard[b]
                        pe = None
                        for i in range(NDB):
                            pe = p.op("tensor", lambda E, o=psQK[b][:],
                                      l=wsl2[:, par, mat, i, :],
                                      r=xn_fm[:, i, tc * 512:(tc + 1) * 512],
                                      st=(i == 0), sp=(i == NDB - 1):
                                      E.matmul(o, l, r, start=st, stop=sp),
                                      waits=w if i == 0 else (),
                                      inc="cP" if i == NDB - 1 else None)
                        ev = p.op("scalar", lambda E, o=dst[:, tc * 512:(tc + 1) * 512],
                                  a=psQK[b][:]: E.copy(o, a),
                                  waits=[("cP", pe)], inc="cS")
                        bank_guard[b] = [("cS", ev)]
                        last = ev
                    return last

                qkv(0, q_sb)
                qkv(1, k_sb)
                vlast = qkv(2, vf_sb)
                for g in range(4 if p2_stage >= 2 else 0):
                    bpar = g % 2
                    tw = [("cS", vlast)] + ptr_guard[bpar]
                    pe = None
                    for t in range(4):
                        sb = 4 * g + t
                        pe = p.op("tensor",
                                  lambda E, o=ptr[bpar][:, t * 128:(t + 1) * 128],
                                  a=vf_sb[:, sb * 128:(sb + 1) * 128]:
                                  E.transpose(o, a, ident[:]),
                                  waits=tw if t == 0 else (),
                                  inc="cP" if t == 3 else None)
                    cp = p.op("scalar", lambda E, o=v_sb[:, 4 * g:4 * g + 4, :],
                              a=ptr[bpar][:, 0:512].rearrange("p (a b) -> p a b", b=128):
                              E.copy(o, a), waits=[("cP", pe)], inc="cS")
                    ptr_guard[bpar] = [("cS", cp)]

                if p2_dump and h == 0 and seq == 0:
                    p.dma("sync", "dmp", qdump_d, q_sb[:], waits=[("cS", vlast)])
                    p.dma("sync", "dmp", kdump_d, k_sb[:], waits=[("cS", vlast)])
                    p.dma("sync", "dmp", xdump_d, xn_fm[:], waits=[("cS", vlast)])
                pv_prev = None
                for tc in range(4 if p2_stage >= 3 else 0):
                    pT_ready = []
                    for qloc in range(4):
                        qb = 4 * tc + qloc
                        s_len = (qb + 1) * 128
                        nchunks = qb // 4 + 1
                        spar = qb % 2
                        evacs = []
                        for c in range(nchunks):
                            cw2 = min(512, s_len - c * 512)
                            b = (qb + c) % 2
                            pe = p.op("tensor", lambda E, o=psQK[b][:, 0:cw2],
                                      l=q_sb[:, qb * 128:(qb + 1) * 128],
                                      r=k_sb[:, c * 512:c * 512 + cw2]:
                                      E.matmul(o, l, r, start=True, stop=True),
                                      waits=bank_guard[b], inc="cP")
                            newg = []
                            if p2_soft >= 0:
                                ev = p.op("scalar", lambda E,
                                          o=s_f[:, c * 512:c * 512 + cw2],
                                          a=psQK[b][:, 0:cw2]:
                                          E.activation(o, a, AF.Copy, scale=ISQ),
                                          waits=[("cP", pe)] + dump_guard, inc="cS")
                                evacs.append(("cS", ev))
                                newg.append(("cS", ev))
                            if c == nchunks - 1 and p2_soft >= 1:
                                mv = p.op("vector", lambda E,
                                          o=s_f[:, s_len - 128:s_len],
                                          a=s_f[:, s_len - 128:s_len], b=maskt[:]:
                                          E.tensor_tensor(o, a, b, ALU.add),
                                          waits=[("cS", ev)], inc="cV")
                                evacs.append(("cV", mv))
                            bank_guard[b] = newg
                        if p2_soft < 2:
                            continue
                        rm = p.op("vector", lambda E, o=smax[:, spar:spar + 1],
                                  a=s_f[:, 0:s_len]:
                                  E.reduce_max(o, a, axis=AX.X, negate=True),
                                  waits=evacs, inc="cV")
                        if p2_soft < 3:
                            continue
                        ex = p.op("scalar", lambda E, o=p_f[:, 0:s_len],
                                  a=s_f[:, 0:s_len], b=smax[:, spar:spar + 1],
                                  acc=ssum[:, spar:spar + 1]:
                                  E.activation(o, a, AF.Exp, bias=b, accum_out=acc),
                                  waits=[("cV", rm)], inc="cS")
                        if p2_soft < 4:
                            continue
                        rv = p.op("vector", lambda E, o=srl[:, spar:spar + 1],
                                  a=ssum[:, spar:spar + 1]: E.reciprocal(o, a),
                                  waits=[("cS", ex)], inc="cV")
                        pb = p.op("vector", lambda E, o=p_b[:, 0:s_len],
                                  a=p_f[:, 0:s_len], s=srl[:, spar:spar + 1]:
                                  E.tensor_single_scalar(o, a, s, ALU.mult),
                                  waits=[("cV", rv)] + dump_guard, inc="cV")
                        if p2_dump and h == 0 and seq == 0 and qb < 4:
                            p.dma("sync", "dmp", sdump_d[qb], s_f[:],
                                  waits=[("cV", pb)])
                            dg = p.dma("sync", "dmp", pdump_d[qb], p_b[:],
                                       waits=[("cV", pb)])
                            dump_guard[:] = [dg]
                        for g in range(((qb + 4) // 4) if p2_stage >= 4 else 0):
                            n_t = min(4, qb + 1 - 4 * g)
                            bpar = g % 2
                            tw = [("cV", pb)] + ptr_guard[bpar]
                            pe = None
                            for t in range(n_t):
                                sb = 4 * g + t
                                pe = p.op("tensor",
                                          lambda E, o=ptr[bpar][:, t * 128:(t + 1) * 128],
                                          a=p_b[:, sb * 128:(sb + 1) * 128]:
                                          E.transpose(o, a, ident[:]),
                                          waits=tw if t == 0 else (),
                                          inc="cP" if t == n_t - 1 else None)
                            cp = p.op("scalar", lambda E,
                                      o=pT[:, 4 * g:4 * g + n_t,
                                           qloc * 128:(qloc + 1) * 128],
                                      a=ptr[bpar][:, 0:n_t * 128].rearrange(
                                          "p (a b) -> p a b", b=128):
                                      E.copy(o, a), waits=[("cP", pe)], inc="cS")
                            ptr_guard[bpar] = [("cS", cp)]
                            pT_ready.append(("cS", cp))
                        for sb in (range(qb + 1, 4 * tc + 4) if p2_stage >= 4 else []):
                            ms = p.op("gpsimd", lambda E,
                                      o=pT[:, sb, qloc * 128:(qloc + 1) * 128]:
                                      E.memset(o, 0.0),
                                      waits=[("cP", pv_prev)] if pv_prev else [],
                                      inc="cG")
                            pT_ready.append(("cG", ms))
                    if p2_stage < 5:
                        continue
                    zpar = tc % 2
                    w = list(pT_ready) + zps_guard[zpar]
                    pe = None
                    for sb in range(4 * tc + 4):
                        pe = p.op("tensor", lambda E, o=psZ2[zpar][:], l=v_sb[:, sb, :],
                                  r=pT[:, sb, :], st=(sb == 0), sp=(sb == 4 * tc + 3):
                                  E.matmul(o, l, r, start=st, stop=sp),
                                  waits=w if sb == 0 else (),
                                  inc="cP" if sb == 4 * tc + 3 else None)
                    pv_prev = pe
                    zc = p.op("scalar", lambda E, o=z_t2[:, zpar, :], a=psZ2[zpar][:]:
                              E.copy(o, a),
                              waits=[("cP", pe)] + zslot_guard[zpar], inc="cS")
                    zps_guard[zpar] = [("cS", zc)]
                    zd = p.dma("sync", f"zo{zpar}", z_d[seq, tc, h], z_t2[:, zpar, :],
                               waits=[("cS", zc)])
                    zslot_guard[zpar] = [zd]
                    z_dma[(seq, tc, h)] = zd
                p.emit()

          if not p2_wo:
              continue
          with ExitStack() as ws:
              zc2 = ws.enter_context(nc.sbuf_tensor(f"zc2_{seq}", [128, 2, NH, 512], BF16))
              wo2 = ws.enter_context(nc.sbuf_tensor(f"wo2_{seq}", [128, 2, NH, 512], BF16))
              xr2 = ws.enter_context(nc.sbuf_tensor(f"xr2_{seq}", [128, 2, 4, 512], F32))
              ou2 = ws.enter_context(nc.sbuf_tensor(f"ou2_{seq}", [128, 2, 4, 512], F32))
              psW4 = [ws.enter_context(nc.psum_tensor(f"psW{i}_{seq}", [128, 512], F32))
                      for i in range(4)]

              p = Plan(k)
              fence(p)
              if seq == 0 and p2_seqs > 1:
                  prefetch_wsl(p, 1, 0)
              wo_dma = {}
              zc_dma = {}
              wo_guard = [[], []]
              zc_guard = [[], []]
              xr_guard = [[], []]
              ou_guard = [[], []]
              bankW_guard = [[] for _ in range(4)]
              wo_dma[0] = p.dma("gpsimd", "wo0", wo2[:, 0, :, :],
                                wo_d[0].rearrange("(i p) j -> p i j", p=128))
              for oc in range(4):
                  if oc + 1 < 4:
                      wo_dma[oc + 1] = p.dma(
                          "gpsimd", f"wo{(oc + 1) % 2}", wo2[:, (oc + 1) % 2, :, :],
                          wo_d[oc + 1].rearrange("(i p) j -> p i j", p=128),
                          waits=wo_guard[(oc + 1) % 2])
                  for tc in range(4):
                      it = oc * 4 + tc
                      zpar = it % 2
                      zc_dma[it] = p.dma("sync", f"zc{zpar}", zc2[:, zpar, :, :],
                                         z_d[seq, tc].rearrange("h p j -> p h j"),
                                         waits=[z_dma[(seq, tc, NH - 1)]] +
                                         zc_guard[zpar])
                      xr_dma = p.dma("scalar", f"xr{zpar}", xr2[:, zpar, :, :],
                                     x_d[seq, tc * 512:(tc + 1) * 512,
                                         oc * 512:(oc + 1) * 512].rearrange(
                                         "(tb p) o -> p tb o", p=128),
                                     waits=xr_guard[zpar])
                      last_add = None
                      last_pe = None
                      for tb in range(4):
                          w = [wo_dma[oc], zc_dma[it]] + bankW_guard[tb]
                          pe = None
                          for hh in range(NH):
                              pe = p.op("tensor", lambda E, o=psW4[tb][:],
                                        l=zc2[:, zpar, hh, tb * 128:(tb + 1) * 128],
                                        r=wo2[:, oc % 2, hh, :],
                                        st=(hh == 0), sp=(hh == NH - 1):
                                        E.matmul(o, l, r, start=st, stop=sp),
                                        waits=w if hh == 0 else (),
                                        inc="cP" if hh == NH - 1 else None)
                          last_pe = pe
                          ad = p.op("vector", lambda E, o=ou2[:, zpar, tb, :],
                                    a=psW4[tb][:], b=xr2[:, zpar, tb, :]:
                                    E.tensor_add(o, a, b),
                                    waits=[("cP", pe), xr_dma] + ou_guard[zpar],
                                    inc="cV")
                          bankW_guard[tb] = [("cV", ad)]
                          last_add = ad
                      od = p.dma("sync", f"x1o{zpar}",
                                 x1_d[seq, tc * 512:(tc + 1) * 512,
                                      oc * 512:(oc + 1) * 512].rearrange(
                                     "(tb p) o -> p tb o", p=128),
                                 ou2[:, zpar, :, :], waits=[("cV", last_add)])
                      ou_guard[zpar] = [od]
                      xr_guard[zpar] = [("cV", last_add)]
                      zc_guard[zpar] = [("cP", last_pe)]
                  wo_guard[oc % 2] = [("cP", k.n["cP"])]
              x1_done[seq] = [(s, k.n[s]) for s in ("x1o0", "x1o1") if s in k.n]
              p.emit()

    if upto == 2:
        return finish(k)

    with ExitStack() as ps:
        x1_sb = ps.enter_context(nc.sbuf_tensor("x1_sb", [128, 4, D], F32))
        xjunk = ps.enter_context(nc.sbuf_tensor("xjunk2", [128, D], F32))
        xn_tm2 = ps.enter_context(nc.sbuf_tensor("xn_tm2b", [128, 2, D], BF16))
        xn2_fm = ps.enter_context(nc.sbuf_tensor("xn2_fm", [128, NDB, 512], BF16))
        h_sb = ps.enter_context(nc.sbuf_tensor("h_sb", [128, NJB, 512], BF16))
        relu2 = ps.enter_context(nc.sbuf_tensor("relu2", [128, 2, 512], F32))
        hsq2 = ps.enter_context(nc.sbuf_tensor("hsq2", [128, 4, 512], BF16))
        wfc2 = ps.enter_context(nc.sbuf_tensor("wfc2", [128, 2, D], BF16))
        wpj2 = ps.enter_context(nc.sbuf_tensor("wpj2", [128, 2, 4, 512], BF16))
        ou2 = ps.enter_context(nc.sbuf_tensor("ou2f", [128, 2, 4, 512], F32))
        sm = [ps.enter_context(nc.sbuf_tensor(f"lnsm{i}b", [128, 2], F32))
              for i in range(6)]
        lrow = ps.enter_context(nc.sbuf_tensor("lrow", [1, 512], F32))
        srow = ps.enter_context(nc.sbuf_tensor("srow", [1, 512], F32))
        trow = ps.enter_context(nc.sbuf_tensor("trow", [1, 512], F32))
        rrow = ps.enter_context(nc.sbuf_tensor("rrow", [1, 512], F32))
        rcol = ps.enter_context(nc.sbuf_tensor("rcol", [128, 4], F32))
        ptrA = ps.enter_context(nc.psum_tensor("ptrAf", [128, 1024], BF16))
        ptrB = ps.enter_context(nc.psum_tensor("ptrBf", [128, 1024], BF16))
        psF = [ps.enter_context(nc.psum_tensor(f"psF{i}", [128, 512], F32))
               for i in range(2)]
        psP = [ps.enter_context(nc.psum_tensor(f"psP{i}", [128, 512], F32))
               for i in range(4)]
        ptr = [ptrA, ptrB]

        wfc_guard = [[], []]
        wpj_guard = [[], []]
        ou_guard = [[], []]
        x1_guard = []

        for c in range(8):
            seq, toff = c // 4, (c % 4) * 512
            p = Plan(k)
            if c == 0:
                fence(p)
            x1v = p.dma("sync", "x1i", x1_sb[:],
                        x1_d[seq, toff:toff + 512, :].rearrange(
                            "(tb p) d -> p tb d", p=128),
                        waits=list(x1_done[seq]) + x1_guard)
            wfc_dma = {}
            wfc_dma[0] = p.dma("scalar", "wf0", wfc2[:, 0, :], wfct_d[0],
                               waits=(list(prep_done) if c == 0 else []) +
                               wfc_guard[0])
            wfc_dma[1] = p.dma("scalar", "wf1", wfc2[:, 1, :], wfct_d[1],
                               waits=wfc_guard[1])

            ptr_guard = [[], []]
            ln_done = ln_block(p, lambda tb, x1v=x1v: (x1_sb[:, tb, :], [x1v]),
                               4, xn2_fm, lambda tb: tb * 128,
                               sm, xjunk, xn_tm2, ptr, ptr_guard, None)

            relu_guard = [[], []]
            bank_guard = [[], []]
            hq_val = {}
            stat_pe = None

            def plan_stat(jj):
                nonlocal stat_pe
                stat_pe = p.op("tensor", lambda E, o=psP[0][0:1, :], l=onesb[:],
                               r=hsq2[:, jj % 4, :], st=(jj == 0), sp=(jj == NJB - 1):
                               E.matmul(o, l, r, start=st, stop=sp),
                               waits=[("cV", hq_val[jj])],
                               inc="cP" if jj == NJB - 1 else None)

            for j in range(NJB):
                b = j % 2
                w = [wfc_dma[j], ("cS", ln_done)] + bank_guard[b]
                pe = None
                wfc_view = wfc2[:, b, :].rearrange("p (i m) -> p i m", m=128)
                for i in range(NDB):
                    pe = p.op("tensor", lambda E, o=psF[b][:],
                              l=wfc_view[:, i, :], r=xn2_fm[:, i, :],
                              st=(i == 0), sp=(i == NDB - 1):
                              E.matmul(o, l, r, start=st, stop=sp),
                              waits=w if i == 0 else (),
                              inc="cP" if i == NDB - 1 else None)
                if j >= 2:
                    plan_stat(j - 2)
                if j + 2 < NJB:
                    wfc_dma[j + 2] = p.dma("scalar", f"wf{b}", wfc2[:, b, :],
                                           wfct_d[j + 2], waits=[("cP", pe)])
                rl = p.op("scalar", lambda E, o=relu2[:, b, :], a=psF[b][:]:
                          E.activation(o, a, AF.Relu),
                          waits=[("cP", pe)] + relu_guard[b], inc="cS")
                bank_guard[b] = [("cS", rl)]
                hb = p.op("vector", lambda E, o=h_sb[:, j, :], a=relu2[:, b, :]:
                          E.tensor_tensor(o, a, a, ALU.mult),
                          waits=[("cS", rl)], inc="cV")
                relu_guard[b] = [("cV", hb)]
                hq_val[j] = p.op("vector", lambda E, o=hsq2[:, j % 4, :], a=h_sb[:, j, :]:
                                 E.tensor_tensor(o, a, a, ALU.mult),
                                 waits=[("cV", hb)], inc="cV")
            plan_stat(NJB - 2)
            plan_stat(NJB - 1)
            c1 = p.op("scalar", lambda E, o=lrow[0:1, :], a=psP[0][0:1, :]:
                      E.activation(o, a, AF.Copy, scale=1.0 / HID),
                      waits=[("cP", stat_pe)], inc="cS")
            c2 = p.op("vector", lambda E, o=lrow[0:1, :], a=lrow[0:1, :]:
                      E.tensor_single_scalar(o, a, F32_EPS, ALU.add),
                      waits=[("cS", c1)], inc="cV")
            ex1 = plan_rsqrt(p, lrow[0:1, :], rrow[0:1, :],
                             srow[0:1, :], trow[0:1, :], [("cV", c2)])
            colpe = None
            for tb in range(4):
                colpe = p.op("tensor", lambda E, o=psP[0][0:128, 508 + tb:509 + tb],
                             l=rrow[0:1, tb * 128:(tb + 1) * 128], r=onesf[0:1, 0:1]:
                             E.matmul(o, l, r, start=True, stop=True),
                             waits=[("cV", ex1)] if tb == 0 else (), inc="cP")
            rc = p.op("vector", lambda E, o=rcol[:], a=psP[0][0:128, 508:512]:
                      E.tensor_copy(o, a), waits=[("cP", colpe)], inc="cV")

            wpj_dma = {}
            bankP_guard = [[] for _ in range(4)]

            def wpj_load(gi, c=c):
                oc, jg = divmod(gi, NJB // 4)
                w = list(wpj_guard[gi % 2])
                if c == 0 and gi < 2:
                    w.extend(prep_done)
                wpj_dma[gi] = p.dma(
                    "scalar", f"wp{gi % 2}", wpj2[:, gi % 2, :, :],
                    wpjt_d[jg * 4:(jg + 1) * 4, :,
                           oc * 512:(oc + 1) * 512].rearrange("j p o -> p j o"),
                    waits=w)

            wpj_load(0)
            wpj_load(1)
            NG = NJB // 4
            for oc in range(4):
                pv = None
                for jg in range(NG):
                    gi = oc * NG + jg
                    for jl in range(4):
                        j = jg * 4 + jl
                        for tb in range(4):
                            w = []
                            if jl == 0 and tb == 0:
                                w += [wpj_dma[gi], ("cV", rc)]
                            if j == 0:
                                w += bankP_guard[tb]
                            pe = p.op("tensor", lambda E, o=psP[tb][:, 0:512],
                                      l=h_sb[:, j, tb * 128:(tb + 1) * 128],
                                      r=wpj2[:, gi % 2, jl, :],
                                      st=(j == 0), sp=(j == NJB - 1):
                                      E.matmul(o, l, r, start=st, stop=sp),
                                      waits=w,
                                      inc="cP" if (jl == 3 and tb == 3) else None)
                            if jl == 3 and tb == 3:
                                pv = pe
                    wpj_guard[gi % 2] = [("cP", pv)]
                    if gi + 2 < 4 * NG:
                        wpj_load(gi + 2)
                opar = oc % 2
                last_ev = None
                for tb in range(4):
                    e1 = p.op("scalar", lambda E, o=ou2[:, opar, tb, :],
                              a=psP[tb][:, 0:512], sc=rcol[:, tb:tb + 1]:
                              E.activation(o, a, AF.Copy, scale=sc),
                              waits=[("cP", pv), ("cV", rc)] + ou_guard[opar],
                              inc="cS")
                    ev = p.op("vector", lambda E, o=ou2[:, opar, tb, :],
                              a=ou2[:, opar, tb, :],
                              b=x1_sb[:, tb, oc * 512:(oc + 1) * 512]:
                              E.tensor_tensor(o, a, b, ALU.add),
                              waits=[("cS", e1)], inc="cV")
                    bankP_guard[tb] = [("cS", e1), ("cV", ev)]
                    last_ev = ev
                od = p.dma("sync", f"yo{opar}", y_d[seq, toff:toff + 512,
                                       oc * 512:(oc + 1) * 512].rearrange(
                           "(tb p) o -> p tb o", p=128), ou2[:, opar, :, :],
                           waits=[("cV", last_ev)])
                ou_guard[opar] = [od]
            x1_guard = [("cV", k.n["cV"])]
            wfc_guard = [[("cP", k.n["cP"])], [("cP", k.n["cP"])]]
            p.emit()

    return finish(k)


_NC_CACHE = None


def kernel(L0_attn_x, L0_attn_Wq, L0_attn_Wk, L0_attn_Wv, L0_attn_Wo,
           L0_ffn_W_fc, L0_ffn_W_proj):
    global _NC_CACHE
    x = np.ascontiguousarray(np.asarray(L0_attn_x))
    wq = np.ascontiguousarray(np.asarray(L0_attn_Wq).T.reshape(D, NH, HD).transpose(1, 0, 2))
    wk = np.ascontiguousarray(np.asarray(L0_attn_Wk).T.reshape(D, NH, HD).transpose(1, 0, 2))
    wv = np.ascontiguousarray(np.asarray(L0_attn_Wv).T.reshape(D, NH, HD).transpose(1, 0, 2))
    wo = np.ascontiguousarray(np.asarray(L0_attn_Wo).T.reshape(D, 4, 512).transpose(1, 0, 2))
    wfc = np.ascontiguousarray(np.asarray(L0_ffn_W_fc))
    wpj = np.ascontiguousarray(np.asarray(L0_ffn_W_proj))
    ident = np.eye(128, dtype=bf16)
    ii, jj = np.indices((128, 128))
    mask = np.where(jj <= ii, np.float32(0), np.float32(NEG)).astype(np.float32)
    onesb = np.ones((128, 1), bf16)
    onesf = np.ones((1, 1), np.float32)
    epsc = np.full((1, 1), F32_EPS, np.float32)

    if _NC_CACHE is None:
        _NC_CACHE = build_nc()
    nc = _NC_CACHE

    in_maps = []
    for c in range(8):
        in_maps.append({
            "x": np.ascontiguousarray(x[2 * c:2 * c + 2]),
            "wq": wq, "wk": wk, "wv": wv, "wo": wo,
            "wfc": wfc, "wpj": wpj,
            "ident": ident, "mask": mask, "onesb": onesb, "onesf": onesf,
            "epsc": epsc,
        })
    res = run_bass_kernel_spmd(nc, in_maps, core_ids=list(range(8)))
    out = np.concatenate([np.asarray(r["y"]).reshape(NSEQ, T, D)
                          for r in res.results], axis=0)
    return out
